# version 11 (final)
# baseline (speedup 1.0000x reference)
import os
os.environ.setdefault("TRNINF_ENABLE_CUSTOMCOMMS_RDH_AG", "1")

import numpy as np
import ml_dtypes
from contextlib import ExitStack

import concourse.bass as bass
import concourse.mybir as mybir
from concourse.bass_utils import run_bass_kernel_spmd

F32 = mybir.dt.float32
BF16 = mybir.dt.bfloat16
AF = mybir.ActivationFunctionType
ALU = mybir.AluOpType
AX = mybir.AxisListType

NCORE = 8
NT = 4096
DM = 4096
H, KVH, D = 32, 8, 128
T = 1024
HT = T + 64
NCH = 8
GN = 16.0
EPS = 1e-5
SCL = float(1.0 / np.sqrt(np.float32(D)))
LN_HALF = float(np.log(0.5))


class Ch:
    def __init__(self, nc, es, name):
        self.h = es.enter_context(nc.semaphore(name))
        self.n = 0


class Eng:
    def __init__(self, ops):
        self.ops = ops

    def wait(self, c, n=None):
        h, v = c.h, (c.n if n is None else n)
        self.ops.append(lambda e: e.wait_ge(h, v))

    def op(self, fn, inc=None, k=1, inc2=None, k2=1):
        if inc is not None:
            inc.n += k
        if inc2 is not None:
            inc2.n += k2
        h = inc.h if inc is not None else None
        h2 = inc2.h if inc2 is not None else None

        def run(e):
            i = fn(e)
            if h is not None:
                i = i.then_inc(h, k)
            if h2 is not None:
                i.then_inc(h2, k2)
        self.ops.append(run)


def g2(ap, g=2):
    return ap.rearrange("p (g d) -> p g d", g=g)


def g4r(ap, g):
    return ap.rearrange("p (g two d) -> p g two d", g=g, two=2)


def qslice(sq, c, hh2, bb):
    base = c * 512 + hh2 * 256
    return sq[:, base:base + 256].rearrange("p (e d) -> p e d", e=2)[:, :, bb * 64:bb * 64 + 64]


def oslice(bank, off):
    return bank[:].rearrange("p (h t) -> p h t", h=4)[:, :, off:off + 64]


BASE = 16640
TAB = BASE + 0
RA = BASE + 16384
RB = BASE + 32768
RC = BASE + 50176
RD = BASE + 68608
RE = BASE + 84992
RF = BASE + 101376
RG = BASE + 117760
RH = BASE + 129024
RSB = BASE + 147456
RSP = BASE + 153856
RSP_END = BASE + 163808
SBUF_END = 228864


def build_program(upto=9, dbg=False, sub5=4):
    nc = bass.Bass("TRN2", target_bir_lowering=False)

    xin = nc.dram_tensor("xin", [HT, DM], F32, kind="ExternalInput").ap()
    wq = nc.dram_tensor("wq", [DM, H * D], F32, kind="ExternalInput").ap()
    wk = nc.dram_tensor("wk", [DM, KVH * D], F32, kind="ExternalInput").ap()
    wv = nc.dram_tensor("wv", [DM, KVH * D], F32, kind="ExternalInput").ap()
    wo = nc.dram_tensor("wo", [H * D, DM], F32, kind="ExternalInput").ap()
    wupT = nc.dram_tensor("wupT", [DM, DM], F32, kind="ExternalInput").ap()
    agw = nc.dram_tensor("agw", [DM, 36], F32, kind="ExternalInput").ap()
    bcat = nc.dram_tensor("bcat", [32, DM], F32, kind="ExternalInput").ap()
    ebias = nc.dram_tensor("ebias", [36, 1], F32, kind="ExternalInput").ap()
    cosin = nc.dram_tensor("cosin", [128, 2 * 9 * 128], F32, kind="ExternalInput").ap()
    identb = nc.dram_tensor("identb", [128, 128], BF16, kind="ExternalInput").ap()
    identf = nc.dram_tensor("identf", [128, 128], F32, kind="ExternalInput").ap()
    glam = nc.dram_tensor("glam", [128, 512], BF16, kind="ExternalInput").ap()
    swm_in = nc.dram_tensor("swm_in", [128, 256], F32, kind="ExternalInput").ap()
    sel_in = nc.dram_tensor("sel_in", [128, 16], F32, kind="ExternalInput").ap()
    zeros_in = nc.dram_tensor("zeros_in", [128, 128], F32, kind="ExternalInput").ap()
    fconst_in = nc.dram_tensor("fconst_in", [128, 4], F32, kind="ExternalInput").ap()
    ones_in = nc.dram_tensor("ones_in", [128, 32], BF16, kind="ExternalInput").ap()
    yout = nc.dram_tensor("y", [T, DM], F32, kind="ExternalOutput").ap()

    xt_hbm = nc.dram_tensor("xt_hbm", [32, 128, HT], BF16, kind="Internal").ap()
    qt_hbm = nc.dram_tensor("qt_hbm", [KVH, 128, NCH * 512], BF16, kind="Internal").ap()
    sqt_hbm = nc.dram_tensor("sqt_hbm", [KVH, 128, NCH * 512], BF16, kind="Internal").ap()
    o_hbm = nc.dram_tensor("o_hbm", [KVH, 128, 4 * T], BF16, kind="Internal").ap()
    r1_hbm = nc.dram_tensor("r1_hbm", [T, DM], F32, kind="Internal").ap()
    p2_hbm = nc.dram_tensor("p2_hbm", [512, DM], F32, kind="Internal").ap()
    cc_in = nc.dram_tensor("cc_in", [1152, 128], F32, kind="Internal").ap()
    cc_out = nc.dram_tensor("cc_out", [NCORE * 1152, 128], F32, kind="Internal").ap()

    es = ExitStack()

    def place(name, off, shape, dt, end=None):
        nb = int(np.prod(shape[1:])) * mybir.dt.size(dt)
        if end is not None:
            assert off + nb <= end, f"{name}: {off}+{nb} > {end}"
        assert off + nb <= SBUF_END, f"{name} past SBUF end"
        return nc.alloc_sbuf_tensor_at(name, shape, dt, offset=off)

    to = [TAB]

    def tab(name, shape, dt):
        t_ = place(name, to[0], shape, dt)
        nb = int(np.prod(shape[1:])) * mybir.dt.size(dt)
        to[0] += (nb + 63) // 64 * 64
        assert to[0] <= RA
        return t_

    cosT = tab("cosT", [128, 9 * 128], F32)
    sinT = tab("sinT", [128, 9 * 128], F32)
    idb = tab("idb", [128, 128], BF16)
    idf = tab("idf", [128, 128], F32)
    glaM = tab("glaM", [128, 512], BF16)
    swM = tab("swM", [128, 256], F32)
    selC = tab("selC", [128, 16], F32)
    zz = tab("zz", [128, 128], F32)
    eb36 = tab("eb36", [36, 1], F32)
    statS = tab("statS", [128, 64], F32)
    statQ = tab("statQ", [128, 64], F32)
    etot = tab("etot", [128, NCH * KVH], F32)
    mstat = tab("mstat", [128, 16], F32)
    sums4 = [tab(f"sums4{i}", [128, 4], F32) for i in range(2)]
    rec4 = [tab(f"rec4{i}", [128, 4], F32) for i in range(2)]
    spc16 = [tab(f"spc16{i}", [128, 4], F32) for i in range(2)]
    qsums = [tab(f"qsums{i}", [128, 4], F32) for i in range(2)]
    qrec = [tab(f"qrec{i}", [128, 4], F32) for i in range(2)]
    nmx = [tab(f"nmx{i}", [128, 2], F32) for i in range(2)]
    sm3 = [tab(f"sm3{i}", [128, 2], F32) for i in range(2)]
    rc3 = [tab(f"rc3{i}", [128, 2], F32) for i in range(2)]
    ppb = tab("ppb", [128, 8], F32)
    pps = tab("pps", [128, 8], F32)
    ecore = tab("ecore", [128, 8], F32)
    eall = tab("eall", [128, 64], F32)
    lgb = tab("lgb", [128, 4], F32)
    lmx = tab("lmx", [128, 1], F32)
    raw4 = tab("raw4", [128, 4], F32)
    nb4 = tab("nb4", [128, 4], F32)
    oh4 = tab("oh4", [128, 4], F32)
    ohx = tab("ohx", [128, 32], BF16)
    fcst = tab("fcst", [128, 4], F32)
    ones32 = tab("ones32", [128, 32], BF16)
    snap1 = tab("snap1", [128, 8], F32)
    snap2 = tab("snap2", [128, 8], F32)
    snap3 = tab("snap3", [128, 8], F32)
    nc.const_aps.register(F32, LN_HALF, fcst[:, 0:1])
    nc.const_aps.register(F32, 1.0, fcst[:, 1:2])
    nc.const_aps.register(F32, EPS, fcst[:, 2:3])

    khatT = place("khatT", RA, [128, KVH * T], BF16, RB)
    krT = place("krT", RB, [128, KVH * HT], BF16, RC)
    vtm = place("vtm", RC, [128, 9 * KVH * 128], BF16, RD)
    eahalf = place("eahalf", RD, [128, NCH * KVH * 128], BF16, RE)
    uarB = place("uarB", RE, [128, NCH * KVH * 128], BF16, RF)
    schunk = place("schunk", RF, [128, NCH * KVH * 128], BF16, RG)

    xbf = [place(f"xbf{i}", RA + i * 8192, [128, DM], BF16, RB) for i in range(2)]
    xst = [place(f"xst{i}", RB + i * 8192, [128, DM], BF16, RB + 16384) for i in range(2)]

    po = [RF]

    def p1t(name, shape, dt):
        t_ = place(name, po[0], shape, dt)
        po[0] += int(np.prod(shape[1:])) * mybir.dt.size(dt)
        assert po[0] <= RH
        return t_

    ebuf = [p1t(f"ebuf{i}", [128, 512], F32) for i in range(2)]
    spbuf = [p1t(f"spbuf{i}", [128, 512], F32) for i in range(2)]
    rotb = p1t("rotb", [128, 512], F32)
    t1b = p1t("t1b", [128, 512], F32)
    krb = p1t("krb", [128, 512], BF16)
    ksb = [p1t(f"ksb{i}", [128, 512], BF16) for i in range(2)]
    cumb = [p1t(f"cumb{i}", [128, 512], F32) for i in range(2)]
    eainv = [p1t(f"eainv{i}", [128, 512], BF16) for i in range(2)]
    decay = [p1t(f"decay{i}", [128, 512], BF16) for i in range(2)]
    ktilb = [p1t(f"ktilb{i}", [128, 512], BF16) for i in range(2)]
    ktiltm = [p1t(f"ktiltm{i}", [128, 512], BF16) for i in range(2)]
    kraw = place("kraw", RH, [128, 9 * 512], F32, RSB)

    sacc = place("sacc", RH, [128, KVH * 128], F32, RSB)
    sinb = place("sinb", RH + 4096, [128, KVH * 128], F32, RSB)
    sj = [place(f"sj{i}", RH + 8192 + i * 4096, [128, KVH * 128], F32, RSB) for i in range(2)]

    xtb = [place(f"xtb{i}", RSB + i * 2176, [128, HT], BF16) for i in range(2)]
    wtb = [place(f"wtb{i}", RSB + 4352 + i * 1024, [128, 512], BF16) for i in range(2)]
    wqb = wtb
    wob = [place(f"wob{i}", RSB + i * 1024, [128, 512], BF16) for i in range(2)]
    xres = [place(f"xres{i}", RSB + 2048 + i * 2048, [128, 512], F32) for i in range(2)]
    r1b = [place(f"r1b{i}", RH + i * 2048, [128, 512], F32, RSB) for i in range(2)]
    wub = [place(f"wub{i}", RSB + i * 1024, [128, 512], BF16) for i in range(2)]
    h1bs = [place(f"h1bs{i}", RSB + 4352 + i * 1024, [128, 512], BF16) for i in range(2)]

    qg = [place(f"qg{i}", RE + i * 2048, [128, 512], F32) for i in range(2)]
    qstage = place("qstage", RE + 4096, [128, 2048], BF16)
    sstage = place("sstage", RE + 8192, [128, 2048], BF16)
    eq_ = [place(f"eq{i}", RSP + i * 2048, [128, 512], F32) for i in range(2)]
    qsb = [place(f"qsb{i}", RG + i * 1024, [128, 512], BF16) for i in range(2)]
    sqf = [place(f"sqf{i}", RG + 2048 + i * 1024, [128, 512], BF16) for i in range(2)]
    rot2 = place("rot2", RG + 4096, [128, 512], F32)
    t12 = place("t12", RG + 6144, [128, 512], F32)

    qtg = [place(f"qtg{i}", RH + i * 8192, [128, NCH * 512], BF16, RSB) for i in range(2)]
    sqg = [place(f"sqg{i}", (RG if i == 0 else RSP), [128, NCH * 512], BF16) for i in range(2)]
    ostage = place("ostage", RE, [128, 4 * T], BF16, RE + 8192)
    ptm = [place(f"ptm{i}", RD + i * 1024, [128, 512], BF16) for i in range(2)]
    scb = [place(f"scb{i}", RD + 2048 + i * 1024, [128, 256], F32) for i in range(2)]
    exb = [place(f"exb{i}", RD + 4096 + i * 1024, [128, 256], F32) for i in range(2)]
    pbf3 = [place(f"pbf3{i}", RD + 6144 + i * 512, [128, 256], BF16) for i in range(2)]
    ptb = [place(f"ptb{i}", RD + 7168 + i * 1024, [128, 512], BF16) for i in range(2)]
    swev = place("swev", RD + 9216, [128, 512], F32, RD + 11264)

    otb = [place(f"otb{i}", RG + i * 2048, [128, T], BF16) for i in range(2)]
    trash4 = place("trash4", RD, [128, 512], F32, RE)
    trash5 = place("trash5", RH + 16384, [128, 512], F32, RSB)

    h1half = place("h1half", RA, [128, 4 * DM], F32, RA + 65536)
    h1T = place("h1T", RE, [128, 32 * 512], BF16, RG)
    rcol = place("rcol", RH, [128, DM], F32, RSB)
    p2b = [place(f"p2b{i}", RG + i * 2048, [128, 512], F32) for i in range(2)]
    agb = place("agb", RG + 4096, [128, 32 * 36], BF16)
    zlog = place("zlog", RG + 6912, [36, 512], F32)
    ohT = place("ohT", RG + 8960, [32, 512], BF16)
    zselT = place("zselT", RG + 9984, [32, 512], BF16)
    bcb = place("bcb", RSP, [32, DM], BF16)
    zst = place("zst", RSP + 8192, [128, 128], F32, RSP_END)

    pb = [es.enter_context(nc.psum_tensor(f"pb{i}", [128, 512], F32)) for i in range(8)]

    def pbf(i):
        return pb[i][:].bitcast(BF16)

    PROG = {k: [] for k in ("tensor", "vector", "scalar", "sync", "gpsimd")}
    tE, vE, sE, yE, gE = (Eng(PROG["tensor"]), Eng(PROG["vector"]),
                          Eng(PROG["scalar"]), Eng(PROG["sync"]), Eng(PROG["gpsimd"]))
    chains = {}

    def ch(name):
        if name not in chains:
            chains[name] = Ch(nc, es, name)
        return chains[name]

    ctab = ch("ctab")
    for dst, src in [(cosT, cosin[:, 0:1152]), (sinT, cosin[:, 1152:2304]),
                     (idb, identb), (idf, identf), (glaM, glam), (swM, swm_in),
                     (selC, sel_in), (zz, zeros_in), (eb36, ebias),
                     (fcst, fconst_in), (ones32, ones_in)]:
        yE.op(lambda y, dst=dst, src=src: y.dma_start(out=dst[:], in_=src), inc=ctab, k=16)

    cx0, ct0, ce0, cw0 = ch("cx0"), ch("ct0"), ch("ce0"), ch("cw0")
    tE.wait(ctab, chains["ctab"].n)
    vE.op(lambda v: v.memset(vtm[64:128, 0:KVH * 128], 0.0))
    vE.op(lambda v: v.memset(zst[:], 0.0))
    for j in range(9):
        m = 64 if j == 8 else 128
        if j >= 2:
            gE.wait(ct0, 8 * (j - 1))
        gE.op(lambda g, j=j, m=m: g.dma_start(out=xbf[j % 2][0:m, :],
                                              in_=xin[j * 128: j * 128 + m, :]),
              inc=ch(f"cx0_{j % 2}"), k=16)
    for j in range(9):
        m = 64 if j == 8 else 128
        tE.wait(ch(f"cx0_{j % 2}"), 16 * (j // 2 + 1))
        for kk in range(8):
            gidx = j * 8 + kk
            if gidx >= 2:
                tE.wait(ce0, gidx - 1)
            for u in range(4):
                tE.op(lambda t, j=j, kk=kk, u=u, m=m:
                      t.transpose(pbf(kk % 2)[:, u * 128: u * 128 + m],
                                  xbf[j % 2][0:m, (kk * 4 + u) * 128:(kk * 4 + u + 1) * 128],
                                  idb[0:m, 0:m]),
                      inc=ct0 if u == 3 else None)
    for j in range(9):
        if j >= 2:
            vE.wait(ch(f"cw0_{j % 2}"), 16 * (j // 2))
        for kk in range(8):
            vE.wait(ct0, j * 8 + kk + 1)
            vE.op(lambda v, j=j, kk=kk:
                  v.tensor_copy(xst[j % 2][:, kk * 512:(kk + 1) * 512], pbf(kk % 2)[:, 0:512]),
                  inc=ce0)
    for j in range(9):
        m = 64 if j == 8 else 128
        yE.wait(ce0, (j + 1) * 8)
        yE.op(lambda y, j=j, m=m:
              y.dma_start(out=xt_hbm[:, :, j * 128: j * 128 + m].rearrange("k p t -> p k t"),
                          in_=xst[j % 2][:].rearrange("p (k t) -> p k t", k=32)[:, :, 0:m]),
              inc=ch(f"cw0_{j % 2}"), k=16)

    cxt1, cwt1, cmm1, cev1 = ch("cxt1"), ch("cwt1"), ch("cmm1"), ch("cev1")
    ckraw, cquse = ch("ckraw"), ch("cquse")
    crot, ckr, ckrt, ckre = ch("crot"), ch("ckr"), ch("ckrt"), ch("ckre")
    cexp, csum, crec, cks = ch("cexp"), ch("csum"), ch("crec"), ch("cks")
    csp, ctp, ccum, cbias, cdec = ch("csp"), ch("ctp"), ch("ccum"), ch("cbias"), ch("cdec")
    cpro, cttm, cetm, cumm, cuev = ch("cpro"), ch("cttm"), ch("cetm"), ch("cumm"), ch("cuev")

    yE.wait(ch("cw0_0"), chains["cw0_0"].n)
    yE.wait(ch("cw0_1"), chains["cw0_1"].n)
    gE.wait(ctab, chains["ctab"].n)
    tE.wait(ce0, chains["ce0"].n)

    GRPS = [(wv, 0), (wv, 512), (wk, 0), (wk, 512)]
    _n1 = [0]
    for gi in range(4):
        wsrc, coff = GRPS[gi]
        for wave in range(2):
            for k in range(32):
                _n1[0] += 1
                n = _n1[0]
                if n > 2:
                    yE.wait(cmm1, n - 2)
                    gE.wait(cmm1, n - 2)
                yE.op(lambda y, k=k, b=(n - 1) % 2:
                      y.dma_start(out=xtb[b][:], in_=xt_hbm[k]),
                      inc=ch(f"cxt1_{(n - 1) % 2}"), k=16)
                gE.op(lambda g, wsrc=wsrc, coff=coff, k=k, b=(n - 1) % 2:
                      g.dma_start(out=wtb[b][:],
                                  in_=wsrc[k * 128:(k + 1) * 128, coff:coff + 512]),
                      inc=ch(f"cwt1_{(n - 1) % 2}"), k=16)

    _n2 = [0]

    def emit_proj(gi, wave):
        for k in range(32):
            _n2[0] += 1
            n = _n2[0]
            tE.wait(ch(f"cxt1_{(n - 1) % 2}"), 16 * ((n - 1) // 2 + 1))
            tE.wait(ch(f"cwt1_{(n - 1) % 2}"), 16 * ((n - 1) // 2 + 1))
            if k == 0:
                for cchk in (cev1, ckraw, cuev, ccum, cpro, cetm, ckre, ctp, cttm):
                    tE.wait(cchk, cchk.n)
            if wave == 0:
                for c in range(NCH):
                    tE.op(lambda t, n=n, k=k, c=c:
                          t.matmul(pb[c][:],
                                   xtb[(n - 1) % 2][:, 64 + c * 128: 64 + (c + 1) * 128],
                                   wtb[(n - 1) % 2][:],
                                   start=(k == 0), stop=(k == 31)),
                          inc=cmm1 if c == NCH - 1 else None)
            else:
                tE.op(lambda t, n=n, k=k:
                      t.matmul(pb[0][0:64, :], xtb[(n - 1) % 2][:, 0:64],
                               wtb[(n - 1) % 2][:],
                               start=(k == 0), stop=(k == 31)),
                      inc=cmm1)
        vE.wait(cmm1, chains["cmm1"].n)
        if gi < 2:
            if wave == 0:
                for c in range(NCH):
                    vE.op(lambda v, gi=gi, c=c:
                          v.tensor_copy(
                              vtm[:, (c + 1) * 1024 + gi * 512:(c + 1) * 1024 + gi * 512 + 512],
                              pb[c][:]), inc=cev1)
            else:
                vE.op(lambda v, gi=gi:
                      v.tensor_copy(vtm[0:64, gi * 512: gi * 512 + 512], pb[0][0:64, :]),
                      inc=cev1)
        else:
            kg = gi - 2
            if wave == 0:
                if kg == 1:
                    vE.wait(ckr, 9)
                    vE.wait(csp, 16)
                    vE.wait(cexp, 8)
                for c in range(NCH):
                    vE.op(lambda v, c=c:
                          v.tensor_copy(kraw[:, (c + 1) * 512:(c + 2) * 512], pb[c][:]),
                          inc=ckraw)
            else:
                vE.op(lambda v: v.tensor_copy(kraw[0:64, 0:512], pb[0][0:64, :]),
                      inc=ckraw)

    def emit_kside(kg):
        vE.wait(ckraw, 9 * (kg + 1))
        sE.wait(ctab, chains["ctab"].n)
        sE.wait(ckraw, 9 * (kg + 1))
        tE.wait(ckraw, 9 * (kg + 1))
        for ci in range(9):
            m = 64 if ci == 0 else 128
            if ci > 0 or kg > 0:
                vE.wait(ckrt)
            vE.op(lambda v, ci=ci, m=m:
                  v.tensor_scalar_mul(g4r(rotb[0:m, :], 4)[:, :, 0, :],
                                      g4r(kraw[0:m, ci * 512:(ci + 1) * 512], 4)[:, :, 1, :],
                                      -1.0))
            vE.op(lambda v, ci=ci, m=m:
                  v.tensor_copy(g4r(rotb[0:m, :], 4)[:, :, 1, :],
                                g4r(kraw[0:m, ci * 512:(ci + 1) * 512], 4)[:, :, 0, :]))
            vE.op(lambda v, ci=ci, m=m:
                  v.tensor_tensor(g2(t1b[0:m, :], 4),
                                  g2(kraw[0:m, ci * 512:(ci + 1) * 512], 4),
                                  cosT[0:m, ci * 128:(ci + 1) * 128].unsqueeze(1)
                                  .broadcast_to((m, 4, 128)), ALU.mult),
                  inc=crot)
            vE.wait(crot)
            vE.op(lambda v, ci=ci, m=m:
                  v.tensor_tensor(g2(rotb[0:m, :], 4), g2(rotb[0:m, :], 4),
                                  sinT[0:m, ci * 128:(ci + 1) * 128].unsqueeze(1)
                                  .broadcast_to((m, 4, 128)), ALU.mult),
                  inc=crot)
            vE.wait(crot)
            vE.op(lambda v, ci=ci, m=m:
                  v.tensor_tensor(krb[0:m, :], t1b[0:m, :], rotb[0:m, :], ALU.add),
                  inc=ckr)
            tE.wait(ckr, chains["ckr"].n)
            if ci == 0:
                tE.wait(cpro, chains["cpro"].n)
                tE.wait(cetm, chains["cetm"].n)
            if ci > 0 or kg > 0:
                tE.wait(ckre)
            for e in range(4):
                tE.op(lambda t, ci=ci, m=m, e=e:
                      t.transpose(pbf(5)[:, e * 128: e * 128 + m],
                                  krb[0:m, e * 128:(e + 1) * 128], idb[0:m, 0:m]),
                      inc=ckrt if e == 3 else None)
            vE.wait(ckrt)
            lo = 0 if ci == 0 else 64 + (ci - 1) * 128
            for e in range(4):
                g = 4 * kg + e
                vE.op(lambda v, g=g, lo=lo, m=m, e=e:
                      v.tensor_copy(krT[:, g * HT + lo: g * HT + lo + m],
                                    pbf(5)[:, e * 128: e * 128 + m]),
                      inc=ckre if e == 3 else None)

        for c in range(NCH):
            ci = c + 1
            p = c % 2
            un = kg * NCH + c
            sE.wait(ckraw, 9 * (kg + 1))
            if un >= 2:
                sE.wait(cks, un - 1)
                sE.wait(ctp, un - 1)
            sE.op(lambda s, ci=ci, p=p:
                  s.activation(ebuf[p][:], kraw[:, ci * 512:(ci + 1) * 512], AF.Exp),
                  inc=cexp)
            sE.op(lambda s, ci=ci, p=p:
                  s.activation(spbuf[p][:], kraw[:, ci * 512:(ci + 1) * 512], AF.Exp,
                               scale=-1.0), inc=csp)
            sE.wait(csp)
            sE.op(lambda s, p=p:
                  s.activation(spbuf[p][:], spbuf[p][:], AF.Ln, bias=1.0),
                  inc=csp)
            vE.wait(cexp, chains["cexp"].n)
            vE.op(lambda v, p=p:
                  v.tensor_reduce(sums4[p][:], g2(ebuf[p][:], 4), AX.X, ALU.add), inc=csum)
            vE.wait(csum)
            vE.op(lambda v, p=p: v.reciprocal(rec4[p][:], sums4[p][:]), inc=crec)
            vE.wait(crec)
            vE.op(lambda v, p=p:
                  v.tensor_tensor(g2(ksb[p][:], 4), g2(ebuf[p][:], 4),
                                  rec4[p][:].unsqueeze(2).broadcast_to((128, 4, 128)),
                                  ALU.mult), inc=cks)
            tE.wait(csp, chains["csp"].n)
            tE.wait(cks, chains["cks"].n)
            if c == 0:
                tE.wait(ckre, chains["ckre"].n)
            if un >= 1:
                tE.wait(ccum, 2 * un)
                tE.wait(cpro, 8 * un)
            for e in range(4):
                dstp = pb[6] if e < 2 else pb[7]
                tE.op(lambda t, p=p, e=e, dstp=dstp:
                      t.transpose(dstp[:, (e % 2) * 128:(e % 2) * 128 + 128],
                                  spbuf[p][:, e * 128:(e + 1) * 128], idf[:]))
                tE.op(lambda t, p=p, e=e:
                      t.transpose(pbf(5)[:, e * 128:(e + 1) * 128],
                                  ksb[p][:, e * 128:(e + 1) * 128], idb[:]),
                      inc=ctp if e == 3 else None)
            vE.wait(ctp, chains["ctp"].n)
            for e in range(4):
                srcp = pb[6] if e < 2 else pb[7]
                vE.op(lambda v, p=p, e=e, srcp=srcp:
                      v.tensor_tensor_scan(cumb[p][:, e * 128:(e + 1) * 128],
                                           srcp[:, (e % 2) * 128:(e % 2) * 128 + 128], zz[:],
                                           0.0, ALU.add, ALU.add),
                      inc=ccum if e % 2 == 1 else None)
            vE.wait(ccum)
            for e in range(4):
                vE.op(lambda v, p=p, e=e:
                      v.tensor_scalar_mul(spc16[p][:, e:e + 1],
                                          cumb[p][:, e * 128 + 127: e * 128 + 128], -1.0 / GN),
                      inc=cbias if e == 3 else None)
            sE.wait(ccum, chains["ccum"].n)
            sE.wait(cbias, chains["cbias"].n)
            if un >= 2:
                sE.wait(cpro, 8 * (un - 1))
            for e in range(4):
                g = 4 * kg + e
                sE.op(lambda s, p=p, e=e, g=g, c=c:
                      s.activation(eahalf[:, (c * 8 + g) * 128:(c * 8 + g + 1) * 128],
                                   cumb[p][:, e * 128:(e + 1) * 128], AF.Exp,
                                   scale=-1.0 / GN, bias=LN_HALF))
                sE.op(lambda s, p=p, e=e:
                      s.activation(eainv[p][:, e * 128:(e + 1) * 128],
                                   cumb[p][:, e * 128:(e + 1) * 128], AF.Exp, scale=1.0 / GN))
                sE.op(lambda s, p=p, e=e:
                      s.activation(decay[p][:, e * 128:(e + 1) * 128],
                                   cumb[p][:, e * 128:(e + 1) * 128], AF.Exp,
                                   scale=1.0 / GN, bias=spc16[p][:, e:e + 1]))
                sE.op(lambda s, p=p, e=e, g=g, c=c:
                      s.activation(etot[:, (c * 8 + g):(c * 8 + g) + 1],
                                   spc16[p][:, e:e + 1], AF.Exp),
                      inc=cdec if e == 3 else None)
            vE.wait(cdec, chains["cdec"].n)
            for e in range(4):
                g = 4 * kg + e
                vE.op(lambda v, p=p, e=e, g=g, c=c:
                      v.tensor_tensor(khatT[:, g * T + c * 128: g * T + (c + 1) * 128],
                                      pbf(5)[:, e * 128:(e + 1) * 128],
                                      eainv[p][:, e * 128:(e + 1) * 128], ALU.mult),
                      inc=cpro)
                vE.op(lambda v, p=p, e=e:
                      v.tensor_tensor(ktilb[p][:, e * 128:(e + 1) * 128],
                                      pbf(5)[:, e * 128:(e + 1) * 128],
                                      decay[p][:, e * 128:(e + 1) * 128], ALU.mult),
                      inc=cpro)
            tE.wait(cpro, chains["cpro"].n)
            if un >= 1:
                tE.wait(cetm, un)
            for e in range(4):
                tE.op(lambda t, p=p, e=e:
                      t.transpose(pbf(5)[:, 512 + e * 128: 512 + (e + 1) * 128],
                                  ktilb[p][:, e * 128:(e + 1) * 128], idb[:]),
                      inc=cttm if e == 3 else None)
            vE.wait(cttm)
            if un >= 2:
                vE.wait(cumm, 4 * (un - 1))
            vE.op(lambda v, p=p: v.tensor_copy(ktiltm[p][:], pbf(5)[:, 512:1024]), inc=cetm)
            tE.wait(cetm, chains["cetm"].n)
            if un >= 1:
                tE.wait(cuev, 4 * un)
            for e in range(4):
                g = 4 * kg + e
                bsel = pb[2] if e < 2 else pb[3]
                tE.op(lambda t, p=p, e=e, g=g, ci=ci, bsel=bsel:
                      t.matmul(bsel[:, (e % 2) * 128:(e % 2) * 128 + 128],
                               ktiltm[p][:, e * 128:(e + 1) * 128],
                               vtm[:, ci * 1024 + g * 128: ci * 1024 + (g + 1) * 128],
                               start=True, stop=True),
                      inc=cumm)
            vE.wait(cumm)
            for e in range(4):
                g = 4 * kg + e
                srcp = pb[2] if e < 2 else pb[3]
                vE.op(lambda v, e=e, g=g, c=c, srcp=srcp:
                      v.tensor_copy(uarB[:, (c * 8 + g) * 128:(c * 8 + g + 1) * 128],
                                    srcp[:, (e % 2) * 128:(e % 2) * 128 + 128]),
                      inc=cuev)

    emit_proj(0, 0); emit_proj(0, 1)
    emit_proj(1, 0); emit_proj(1, 1)
    emit_proj(2, 0); emit_proj(2, 1)
    emit_kside(0)
    emit_proj(3, 0); emit_proj(3, 1)
    emit_kside(1)

    crl, cdma, ccc, cschunk = ch("crl"), ch("cdma"), ch("ccc"), ch("cschunk")
    vE.wait(cuev, chains["cuev"].n)
    vE.wait(cdec, chains["cdec"].n)
    vE.op(lambda v: v.tensor_copy(sacc[:], uarB[:, 0:KVH * 128]))
    vE.op(lambda v: v.tensor_copy(ecore[:], etot[:, 0:8]), inc=crl)
    for c in range(1, NCH):
        vE.wait(crl)
        for g in range(KVH):
            vE.op(lambda v, c=c, g=g:
                  v.scalar_tensor_tensor(sacc[:, g * 128:(g + 1) * 128],
                                         sacc[:, g * 128:(g + 1) * 128],
                                         etot[:, (c * 8 + g):(c * 8 + g) + 1],
                                         uarB[:, (c * 8 + g) * 128:(c * 8 + g + 1) * 128],
                                         ALU.mult, ALU.add),
                  inc=crl if g == KVH - 1 else None)
        vE.wait(crl)
        vE.op(lambda v, c=c:
              v.tensor_tensor(ecore[:], ecore[:], etot[:, c * 8:(c + 1) * 8], ALU.mult),
              inc=crl)
    yE.wait(crl, chains["crl"].n)
    yE.op(lambda y: y.dma_start(
        out=cc_in[0:1024, :].rearrange("(g s) d -> g s d", g=KVH).transpose([1, 0, 2]),
        in_=sacc[:].rearrange("s (g d) -> s g d", g=KVH)), inc=cdma, k=16)
    yE.op(lambda y: y.dma_start(out=cc_in[1024:1152, 0:8], in_=ecore[:]),
          inc=cdma, k=16)
    yE.op(lambda y: y.dma_start(out=cc_in[1024:1152, 8:128], in_=zz[:, 0:120]),
          inc=cdma, k=16)
    gE.wait(cdma, chains["cdma"].n)
    gE.op(lambda g: g.collective_compute("AllGather", ALU.bypass,
                                         replica_groups=[list(range(NCORE))],
                                         ins=[cc_in], outs=[cc_out]), inc=ccc)
    yE.wait(ccc, chains["ccc"].n)
    ceall = ch("ceall")
    for j8 in range(8):
        yE.op(lambda y, j8=j8:
              y.dma_start(out=eall[:, j8 * 8:(j8 + 1) * 8],
                          in_=cc_out[j8 * 1152 + 1024: j8 * 1152 + 1152, 0:8]),
              inc=ceall, k=16)
    sj_gate = []
    for g in range(KVH):
        vE.op(lambda v, g=g: v.tensor_copy(sinb[:, g * 128:(g + 1) * 128], zz[:]))
    vE.op(lambda v: v.tensor_scalar(ppb[:], etot[:, 0:8], 0.0, 1.0, ALU.mult, ALU.add),
          inc=crl)
    CRL_START = chains["crl"].n
    for idx, j in enumerate(range(6, -1, -1)):
        if idx >= 2:
            yE.wait(crl, CRL_START + 4 * (idx - 1))
        yE.op(lambda y, j=j, b=idx % 2:
              y.dma_start(out=sj[b][:].rearrange("s (g d) -> s g d", g=KVH),
                          in_=cc_out[j * 1152: j * 1152 + 1024, :]
                          .rearrange("(g s) d -> g s d", g=KVH).transpose([1, 0, 2])),
              inc=ch(f"csj_{idx % 2}"), k=16)
    vE.wait(ceall, chains["ceall"].n)
    for idx, j in enumerate(range(6, -1, -1)):
        vE.wait(ch(f"csj_{idx % 2}"), 16 * (idx // 2 + 1))
        vE.wait(crl)
        vE.op(lambda v, j=j:
              v.tensor_scalar(pps[:], eall[:, (j + 1) * 8:(j + 2) * 8],
                              selC[:, j + 1:j + 2], selC[:, 9 + j:10 + j],
                              ALU.mult, ALU.add), inc=crl)
        vE.wait(crl)
        vE.op(lambda v: v.tensor_tensor(ppb[:], ppb[:], pps[:], ALU.mult), inc=crl)
        vE.wait(crl)
        vE.op(lambda v, j=j: v.tensor_scalar_mul(pps[:], ppb[:], selC[:, j:j + 1]), inc=crl)
        vE.wait(crl)
        for g in range(KVH):
            vE.op(lambda v, b=idx % 2, g=g:
                  v.scalar_tensor_tensor(sinb[:, g * 128:(g + 1) * 128],
                                         sj[b][:, g * 128:(g + 1) * 128],
                                         pps[:, g:g + 1],
                                         sinb[:, g * 128:(g + 1) * 128],
                                         ALU.mult, ALU.add),
                  inc=crl if g == KVH - 1 else None)
        vE.wait(crl)
    vE.op(lambda v: v.tensor_copy(sacc[:], sinb[:]), inc=crl)
    for c in range(NCH):
        vE.wait(crl)
        for g in range(KVH):
            vE.op(lambda v, c=c, g=g:
                  v.tensor_copy(schunk[:, (c * 8 + g) * 128:(c * 8 + g + 1) * 128],
                                sacc[:, g * 128:(g + 1) * 128]), inc=cschunk)
        vE.wait(cschunk)
        if c < NCH - 1:
            for g in range(KVH):
                vE.op(lambda v, c=c, g=g:
                      v.scalar_tensor_tensor(sacc[:, g * 128:(g + 1) * 128],
                                             sacc[:, g * 128:(g + 1) * 128],
                                             etot[:, (c * 8 + g):(c * 8 + g) + 1],
                                             uarB[:, (c * 8 + g) * 128:(c * 8 + g + 1) * 128],
                                             ALU.mult, ALU.add),
                      inc=crl if g == KVH - 1 else None)
    if upto >= 3:

        cxt2, cwq, cqmm, cqev = ch("cxt2"), ch("cwq"), ch("cqmm"), ch("cqev")
        cqs, cqsum, cqrec, cqks = ch("cqs"), ch("cqsum"), ch("cqrec"), ch("cqks")
        cqrot, cqkr, cqtp, cqst, cqw = ch("cqrot"), ch("cqkr"), ch("cqtp"), ch("cqst"), ch("cqw")

        for cchk in (cev1, ckraw, cuev, ccum, cpro, cetm, ckre, ctp, cttm):
            tE.wait(cchk, cchk.n)
        _n3 = [0]
        for g in range(KVH):
            for k in range(32):
                _n3[0] += 1
                n = _n3[0]
                if n > 2:
                    yE.wait(cqmm, n - 2)
                    gE.wait(cqmm, n - 2)
                yE.op(lambda y, k=k, b=(n - 1) % 2:
                      y.dma_start(out=xtb[b][:], in_=xt_hbm[k]),
                      inc=ch(f"cxt2_{(n - 1) % 2}"), k=16)
                gE.op(lambda gg, g=g, k=k, b=(n - 1) % 2:
                      gg.dma_start(out=wqb[b][:],
                                   in_=wq[k * 128:(k + 1) * 128, g * 512:(g + 1) * 512]),
                      inc=ch(f"cwq_{(n - 1) % 2}"), k=16)
        _n4 = [0]
        for g in range(KVH):
            for k in range(32):
                _n4[0] += 1
                n = _n4[0]
                tE.wait(ch(f"cxt2_{(n - 1) % 2}"), 16 * ((n - 1) // 2 + 1))
                tE.wait(ch(f"cwq_{(n - 1) % 2}"), 16 * ((n - 1) // 2 + 1))
                if k == 0:
                    tE.wait(cqst, chains["cqst"].n)
                for c in range(NCH):
                    tE.op(lambda t, k=k, c=c, b=(n - 1) % 2:
                          t.matmul(pb[c][:],
                                   xtb[b][:, 64 + c * 128: 64 + (c + 1) * 128],
                                   wqb[b][:], start=(k == 0), stop=(k == 31)),
                          inc=cqmm if c == NCH - 1 else None)
            for half in range(2):
                for c4 in range(4):
                    c = half * 4 + c4
                    p = c % 2
                    un = g * NCH + c
                    vE.wait(cqmm, chains["cqmm"].n)
                    if c4 == 0:
                        vE.wait(cqw, chains["cqw"].n)
                    if un >= 2:
                        vE.wait(cqs, un - 1)
                        vE.wait(cqtp, un - 1)
                    vE.op(lambda v, c=c, p=p:
                          v.tensor_copy(qg[p][:], pb[c][:]), inc=cqev)
                    sE.wait(cqev, chains["cqev"].n)
                    if un >= 2:
                        sE.wait(cqks, un - 1)
                    sE.op(lambda s, p=p:
                          s.activation(eq_[p][:], qg[p][:], AF.Exp), inc=cqs)
                    vE.wait(cqs, chains["cqs"].n)
                    vE.op(lambda v, p=p:
                          v.tensor_reduce(qsums[p][:], g2(eq_[p][:], 4), AX.X, ALU.add),
                          inc=cqsum)
                    vE.wait(cqsum)
                    vE.op(lambda v, p=p: v.reciprocal(qrec[p][:], qsums[p][:]), inc=cqrec)
                    vE.wait(cqrec)
                    vE.op(lambda v, p=p:
                          v.tensor_tensor(g2(qsb[p][:], 4), g2(eq_[p][:], 4),
                                          qrec[p][:].unsqueeze(2).broadcast_to((128, 4, 128)),
                                          ALU.mult), inc=cqks)
                    vE.op(lambda v, p=p:
                          v.tensor_scalar_mul(g4r(rot2[:], 4)[:, :, 0, :],
                                              g4r(qg[p][:], 4)[:, :, 1, :], -1.0))
                    vE.op(lambda v, p=p:
                          v.tensor_copy(g4r(rot2[:], 4)[:, :, 1, :], g4r(qg[p][:], 4)[:, :, 0, :]))
                    vE.op(lambda v, p=p, c=c:
                          v.tensor_tensor(g2(t12[:], 4), g2(qg[p][:], 4),
                                          cosT[:, (c + 1) * 128:(c + 2) * 128].unsqueeze(1)
                                          .broadcast_to((128, 4, 128)), ALU.mult), inc=cqrot)
                    vE.wait(cqrot)
                    vE.op(lambda v, c=c:
                          v.tensor_tensor(g2(rot2[:], 4), g2(rot2[:], 4),
                                          sinT[:, (c + 1) * 128:(c + 2) * 128].unsqueeze(1)
                                          .broadcast_to((128, 4, 128)), ALU.mult), inc=cqrot)
                    vE.wait(cqrot)
                    vE.op(lambda v, p=p:
                          v.tensor_tensor(sqf[p][:], t12[:], rot2[:], ALU.add), inc=cqkr)
                    tE.wait(cqks, chains["cqks"].n)
                    tE.wait(cqkr, chains["cqkr"].n)
                    for e in range(4):
                        tE.op(lambda t, p=p, e=e, c=c:
                              t.transpose(pbf(c)[:, e * 128:(e + 1) * 128],
                                          qsb[p][:, e * 128:(e + 1) * 128], idb[:]))
                        tE.op(lambda t, p=p, e=e, c=c:
                              t.transpose(pbf(c)[:, 512 + e * 128: 512 + (e + 1) * 128],
                                          sqf[p][:, e * 128:(e + 1) * 128], idb[:]),
                              inc=cqtp if e == 3 else None)
                    vE.wait(cqtp, chains["cqtp"].n)
                    for e in range(4):
                        vE.op(lambda v, c=c, c4=c4, e=e, g=g:
                              v.tensor_tensor(qstage[:, c4 * 512 + e * 128: c4 * 512 + (e + 1) * 128],
                                              pbf(c)[:, e * 128:(e + 1) * 128],
                                              eahalf[:, (c * 8 + g) * 128:(c * 8 + g + 1) * 128],
                                              ALU.mult), inc=cqst)
                        vE.op(lambda v, c4=c4, e=e, c=c:
                              v.tensor_scalar_mul(
                                  sstage[:, c4 * 512 + e * 128: c4 * 512 + (e + 1) * 128],
                                  pbf(c)[:, 512 + e * 128: 512 + (e + 1) * 128], SCL),
                              inc=cqst)
                sE.wait(cqst, chains["cqst"].n)
                sE.op(lambda s, g=g, half=half:
                      s.dma_start(out=qt_hbm[g][:, half * 2048:(half + 1) * 2048],
                                  in_=qstage[:]), inc=cqw, k=16)
                sE.op(lambda s, g=g, half=half:
                      s.dma_start(out=sqt_hbm[g][:, half * 2048:(half + 1) * 2048],
                                  in_=sstage[:]), inc=cqw, k=16)
    if upto >= 4:

        cql = ch("cql")
        cptmm, cptm, comm, cpv = ch("cptmm"), ch("cptm"), ch("comm"), ch("cpv")
        csc, cscev, cmax3, cexp3, crec3, cpbf3 = (ch("csc"), ch("cscev"), ch("cmax3"),
                                                  ch("cexp3"), ch("crec3"), ch("cpbf3"))
        cptr3, cpte3, coev, cow = ch("cptr3"), ch("cpte3"), ch("coev"), ch("cow")
        cswev = ch("cswev")

        for g in range(KVH):
            yE.wait(cqw, chains["cqw"].n)
            if g >= 2:
                yE.wait(coev, (g - 1) * NCH)
            yE.op(lambda y, g=g: y.dma_start(out=qtg[g % 2][:], in_=qt_hbm[g]),
                  inc=ch(f"cql_{g % 2}"), k=16)
            yE.op(lambda y, g=g: y.dma_start(out=sqg[g % 2][:], in_=sqt_hbm[g]),
                  inc=ch(f"cql_{g % 2}"), k=16)
            tE.wait(ch(f"cql_{g % 2}"), 32 * (g // 2 + 1))
            if g == 0:
                tE.wait(cschunk, chains["cschunk"].n)
                tE.wait(cqst, chains["cqst"].n)
            if g >= 1:
                tE.wait(cow, 16 * g)
            for c in range(NCH):
                un = g * NCH + c
                p = un % 2
                if un >= 2:
                    tE.wait(cptm, un - 1)
                tE.op(lambda t, g=g, c=c, p=p:
                      t.matmul(pb[2 + p][:], khatT[:, g * T + c * 128: g * T + (c + 1) * 128],
                               qtg[g % 2][:, c * 512:(c + 1) * 512], start=True, stop=True),
                      inc=cptmm)
                vE.wait(cptmm, chains["cptmm"].n)
                if un >= 2:
                    vE.wait(comm, 2 * (un - 1))
                vE.op(lambda v, p=p:
                      v.tensor_tensor(ptm[p][:], pb[2 + p][:], glaM[:], ALU.mult), inc=cptm)
                tE.wait(cptm, chains["cptm"].n)
                if un >= 2:
                    tE.wait(coev, un - 1)
                tE.op(lambda t, g=g, c=c, p=p:
                      t.matmul(pb[p][:], schunk[:, (c * 8 + g) * 128:(c * 8 + g + 1) * 128],
                               qtg[g % 2][:, c * 512:(c + 1) * 512], start=True, stop=False),
                      inc=comm)
                tE.op(lambda t, g=g, c=c, p=p:
                      t.matmul(pb[p][:],
                               vtm[:, (c + 1) * 1024 + g * 128:(c + 1) * 1024 + (g + 1) * 128],
                               ptm[p][:], start=False, stop=True),
                      inc=comm)
                for bb in range(2):
                    bidx = 2 * c + bb
                    sun = un * 2 + bb
                    pp = sun % 2
                    if sun >= 2:
                        tE.wait(cscev, sun - 1)
                    for hh2 in range(2):
                        for e2 in range(2):
                            qoff = c * 512 + (hh2 * 2 + e2) * 128 + bb * 64
                            tE.op(lambda t, g=g, bb=bb, hh2=hh2, e2=e2, pp=pp,
                                  bidx=bidx, qoff=qoff:
                                  t.matmul(pb[4 + pp][e2 * 64:(e2 + 1) * 64,
                                                      hh2 * 128:(hh2 + 1) * 128],
                                           sqg[g % 2][:, qoff:qoff + 64],
                                           krT[:, g * HT + bidx * 64: g * HT + bidx * 64 + 128],
                                           start=True, stop=True),
                                  inc=csc if (hh2 == 1 and e2 == 1) else None)
                    vE.wait(csc, chains["csc"].n)
                    if sun >= 2:
                        vE.wait(cexp3, 2 * (sun - 1))
                    moff = 0 if bidx == 0 else 128
                    vE.op(lambda v, pp=pp, moff=moff:
                          v.tensor_tensor(g2(scb[pp][:]), g2(pb[4 + pp][:, 0:256]),
                                          swM[:, moff:moff + 128].unsqueeze(1)
                                          .broadcast_to((128, 2, 128)), ALU.add),
                          inc=cscev)
                    vE.wait(cscev)
                    vE.op(lambda v, pp=pp:
                          v.tensor_reduce(nmx[pp][:], g2(scb[pp][:]), AX.X, ALU.max,
                                          negate=True), inc=cmax3)
                    sE.wait(cmax3, chains["cmax3"].n)
                    if sun >= 2:
                        sE.wait(cpbf3, 2 * (sun - 1))
                    for e in range(2):
                        sE.op(lambda s, pp=pp, e=e:
                              s.activation(exb[pp][:, e * 128:(e + 1) * 128],
                                           scb[pp][:, e * 128:(e + 1) * 128], AF.Exp,
                                           bias=nmx[pp][:, e:e + 1],
                                           accum_out=sm3[pp][:, e:e + 1]),
                              inc=cexp3)
                    vE.wait(cexp3, chains["cexp3"].n)
                    vE.op(lambda v, pp=pp: v.reciprocal(rc3[pp][:], sm3[pp][:]), inc=crec3)
                    vE.wait(crec3)
                    for e in range(2):
                        vE.op(lambda v, pp=pp, e=e:
                              v.tensor_scalar(pbf3[pp][:, e * 128:(e + 1) * 128],
                                              exb[pp][:, e * 128:(e + 1) * 128],
                                              rc3[pp][:, e:e + 1], 0.5, ALU.mult, ALU.mult),
                              inc=cpbf3)
                    tE.wait(cpbf3, chains["cpbf3"].n)
                    if sun >= 1:
                        tE.wait(cpte3, sun)
                    if bb == 0:
                        lopart = 0 if c == 0 else 64
                        for e in range(2):
                            tE.op(lambda t, pp=pp, e=e, lopart=lopart:
                                  t.transpose(pbf(6)[lopart:lopart + 64, e * 128:(e + 1) * 128],
                                              pbf3[pp][:, e * 128: e * 128 + 64], idb[:]))
                            tE.op(lambda t, pp=pp, e=e:
                                  t.transpose(pbf(6)[0:64, 256 + e * 128: 256 + (e + 1) * 128],
                                              pbf3[pp][:, e * 128 + 64: (e + 1) * 128], idb[:]),
                                  inc=cptr3 if e == 1 else None)
                    else:
                        for e in range(2):
                            tE.op(lambda t, pp=pp, e=e:
                                  t.transpose(pbf(6)[:, e * 128:(e + 1) * 128],
                                              pbf3[pp][:, e * 128:(e + 1) * 128], idb[:]),
                                  inc=cptr3 if e == 1 else None)
                    vE.wait(cptr3, chains["cptr3"].n)
                    if sun >= 2:
                        vE.wait(cpv, sun - 1)
                    if bb == 0:
                        lopart = 0 if c == 0 else 64
                        vE.op(lambda v, pp=pp: v.memset(ptb[pp][:], 0.0))
                        vE.op(lambda v, pp=pp, lopart=lopart:
                              v.tensor_copy(ptb[pp][lopart:lopart + 64, 0:256],
                                            pbf(6)[lopart:lopart + 64, 0:256]))
                        vE.op(lambda v, pp=pp:
                              v.tensor_copy(ptb[pp][0:64, 256:512],
                                            pbf(6)[0:64, 256:512]),
                              inc=cpte3)
                    else:
                        vE.op(lambda v, pp=pp:
                              v.tensor_copy(ptb[pp][:], pbf(6)[:, 0:512]),
                              inc=cpte3)
                    tE.wait(cpte3, chains["cpte3"].n)
                    if bb == 0:
                        tE.wait(cswev)
                        tE.op(lambda t, g=g, c=c, pp=pp:
                              t.matmul(pb[7][:, 0:256],
                                       vtm[:, c * 1024 + g * 128: c * 1024 + (g + 1) * 128],
                                       ptb[pp][:, 0:256],
                                       start=True, stop=False))
                        tE.op(lambda t, g=g, c=c, pp=pp:
                              t.matmul(pb[7][:, 0:256],
                                       vtm[:, (c + 1) * 1024 + g * 128:(c + 1) * 1024 + (g + 1) * 128],
                                       ptb[pp][:, 256:512],
                                       start=False, stop=False),
                              inc=cpv)
                    else:
                        tE.op(lambda t, g=g, c=c, pp=pp:
                              t.matmul(pb[7][:, 256:512],
                                       vtm[:, (c + 1) * 1024 + g * 128:(c + 1) * 1024 + (g + 1) * 128],
                                       ptb[pp][:, 0:256],
                                       start=False, stop=True),
                              inc=cpv)
                vE.wait(cpv, chains["cpv"].n)
                if c == 0 and g >= 1:
                    vE.wait(cow, 16 * g)
                vE.op(lambda v: v.tensor_copy(swev[:], pb[7][:]), inc=cswev)
                vE.wait(cswev)
                for bb in range(2):
                    vE.op(lambda v, c=c, p=p, bb=bb:
                          v.tensor_tensor(
                              ostage[:].rearrange("p (h t) -> p h t", h=4)
                              [:, :, c * 128 + bb * 64: c * 128 + bb * 64 + 64],
                              pb[p][:].rearrange("p (h t) -> p h t", h=4)
                              [:, :, bb * 64: bb * 64 + 64],
                              swev[:, bb * 256:(bb + 1) * 256]
                              .rearrange("p (h a) -> p h a", h=4), ALU.add),
                          inc=coev if bb == 1 else None)
            yE.wait(coev, chains["coev"].n)
            yE.op(lambda y, g=g: y.dma_start(out=o_hbm[g], in_=ostage[:]), inc=cow, k=16)
    if upto >= 5:

        cwo, cot, cwomm, cxr, cr1, cst4, cr1w = (ch("cwo"), ch("cot"), ch("cwomm"), ch("cxr"),
                                                 ch("cr1"), ch("cst4"), ch("cr1w"))
        tE.wait(coev, chains["coev"].n)
        tE.wait(cptm, chains["cptm"].n)
        tE.wait(cscev, chains["cscev"].n)
        tE.wait(cpte3, chains["cpte3"].n)
        yE.wait(cow, chains["cow"].n)
        _n5 = [0]
        for go in range(8):
            for k in range(32):
                _n5[0] += 1
                n = _n5[0]
                if n > 2:
                    gE.wait(cwomm, n - 2)
                    yE.wait(cwomm, n - 2)
                gE.op(lambda g, go=go, k=k, b=(n - 1) % 2:
                      g.dma_start(out=wob[b][:],
                                  in_=wo[k * 128:(k + 1) * 128, go * 512:(go + 1) * 512]),
                      inc=ch(f"cwo_{(n - 1) % 2}"), k=16)
                yE.op(lambda y, k=k, b=(n - 1) % 2:
                      y.dma_start(out=otb[b][:],
                                  in_=o_hbm[k // 4][:, (k % 4) * T:(k % 4 + 1) * T]),
                      inc=ch(f"cot_{(n - 1) % 2}"), k=16)
            for c in range(NCH):
                nn = go * 8 + c
                if nn >= 2:
                    yE.wait(cr1, nn - 1)
                yE.op(lambda y, go=go, c=c, b=nn % 2:
                      y.dma_start(out=xres[b][:],
                                  in_=xin[64 + c * 128: 64 + (c + 1) * 128,
                                          go * 512:(go + 1) * 512]),
                      inc=ch(f"cxr_{nn % 2}"), k=16)
            for k in range(32):
                nk = go * 32 + k
                tE.wait(ch(f"cwo_{nk % 2}"), 16 * (nk // 2 + 1))
                tE.wait(ch(f"cot_{nk % 2}"), 16 * (nk // 2 + 1))
                if k == 0 and go >= 1:
                    tE.wait(cr1, go * 8)
                for c in range(NCH):
                    tE.op(lambda t, k=k, c=c, b=nk % 2:
                          t.matmul(pb[c][:], otb[b][:, c * 128:(c + 1) * 128],
                                   wob[b][:], start=(k == 0), stop=(k == 31)),
                          inc=cwomm if c == NCH - 1 else None)
            vE.wait(cwomm, chains["cwomm"].n)
            for c in range(NCH):
                nn = go * 8 + c
                vE.wait(ch(f"cxr_{nn % 2}"), 16 * (nn // 2 + 1))
                if nn >= 2:
                    if sub5 >= 4 and sub5 != 6:
                        vE.wait(ch(f"cr1w_{nn % 2}"), 16 * (nn // 2))
                    if sub5 >= 2:
                        vE.wait(cst4, (2 if sub5 >= 3 else 1) * (nn - 1))
                vE.op(lambda v, c=c, b=nn % 2:
                      v.tensor_tensor(r1b[b][:], pb[c][:], xres[b][:], ALU.add), inc=cr1)
                vE.wait(cr1)
                if sub5 >= 2:
                    vE.op(lambda v, go=go, c=c, b=nn % 2:
                          v.tensor_reduce(statS[:, (c * 8 + go):(c * 8 + go) + 1], r1b[b][:],
                                          AX.X, ALU.add), inc=cst4)
                if sub5 >= 3:
                    sE.wait(cr1, chains["cr1"].n)
                    sE.op(lambda s, go=go, c=c, b=nn % 2:
                          s.activation(trash4[:], r1b[b][:], AF.Square,
                                       accum_out=statQ[:, (c * 8 + go):(c * 8 + go) + 1]),
                          inc=cst4)
                if sub5 >= 4:
                    if sub5 != 6:
                        sE.wait(cst4, chains["cst4"].n)
                    sE.op(lambda s, go=go, c=c, b=nn % 2:
                          s.dma_start(out=r1_hbm[c * 128:(c + 1) * 128, go * 512:(go + 1) * 512],
                                      in_=r1b[b][:]), inc=ch(f"cr1w_{nn % 2}"), k=16)
    if upto >= 6:

        cr1l, c5s, c5r, c5h, c5b, c5t, c5ht = (ch("cr1l"), ch("c5s"), ch("c5r"), ch("c5h"),
                                               ch("c5b"), ch("c5t"), ch("c5ht"))
        cag, czmm, czl, coh, czsel = ch("cag"), ch("czmm"), ch("czl"), ch("coh"), ch("czsel")
        cwu, cfmm, cp2e, cp2w, cp2l, cst5, c5o, cfin = (ch("cwu"), ch("cfmm"), ch("cp2e"),
                                                        ch("cp2w"), ch("cp2l"), ch("cst5"),
                                                        ch("c5o"), ch("cfin"))
        gE.wait(cpv, chains["cpv"].n)
        gE.op(lambda g: g.dma_start(
            out=agb[:].rearrange("p (k j) -> p k j", k=32),
            in_=agw[:].rearrange("(k p) j -> p k j", k=32)), inc=cag, k=16)
        gE.op(lambda g: g.dma_start(out=bcb[:], in_=bcat), inc=cag, k=16)

        def ln_stats(mb, cc):
            vE.op(lambda v, cc=cc, mb=mb:
                  v.tensor_reduce(mstat[:, mb:mb + 1], statS[:, cc * 8:(cc + 1) * 8],
                                  AX.X, ALU.add))
            vE.op(lambda v, cc=cc, mb=mb:
                  v.tensor_reduce(mstat[:, mb + 1:mb + 2], statQ[:, cc * 8:(cc + 1) * 8],
                                  AX.X, ALU.add), inc=c5s)
            vE.wait(c5s)
            vE.op(lambda v, mb=mb:
                  v.tensor_scalar_mul(mstat[:, mb:mb + 1], mstat[:, mb:mb + 1], 1.0 / DM))
            vE.op(lambda v, mb=mb:
                  v.tensor_scalar_mul(mstat[:, mb + 1:mb + 2], mstat[:, mb + 1:mb + 2],
                                      1.0 / DM), inc=c5s)
            vE.wait(c5s)
            vE.op(lambda v, mb=mb:
                  v.tensor_scalar(mstat[:, mb + 2:mb + 3], mstat[:, mb:mb + 1],
                                  mstat[:, mb:mb + 1], None, ALU.mult), inc=c5s)
            vE.wait(c5s)
            vE.op(lambda v, mb=mb:
                  v.tensor_tensor(mstat[:, mb + 2:mb + 3], mstat[:, mb + 1:mb + 2],
                                  mstat[:, mb + 2:mb + 3], ALU.subtract), inc=c5s)
            sE.wait(c5s, chains["c5s"].n)
            sE.op(lambda s, mb=mb:
                  s.activation(mstat[:, mb + 2:mb + 3], mstat[:, mb + 2:mb + 3], AF.Ln,
                               bias=EPS), inc=c5s)
            sE.wait(c5s, chains["c5s"].n)
            sE.op(lambda s, mb=mb:
                  s.activation(mstat[:, mb + 1:mb + 2], mstat[:, mb + 2:mb + 3], AF.Exp,
                               scale=-0.5), inc=c5r)
            vE.wait(c5r, chains["c5r"].n)
            vE.op(lambda v, mb=mb:
                  v.tensor_scalar(mstat[:, mb + 3:mb + 4], mstat[:, mb:mb + 1],
                                  mstat[:, mb + 1:mb + 2], -1.0, ALU.mult, ALU.mult),
                  inc=c5s)

        for hf in range(2):
            if hf == 0:
                tE.wait(cr1, chains["cr1"].n)
            else:
                vE.wait(c5o, chains["c5o"].n)
                vE.wait(cfmm, chains["cfmm"].n)
                sE.wait(cp2e, chains["cp2e"].n)
                sE.wait(c5o, chains["c5o"].n)
            for c4 in range(4):
                cc = hf * 4 + c4
                idx5 = hf * 4 + c4
                yE.wait(ch("cr1w_0"), chains["cr1w_0"].n)
                yE.wait(ch("cr1w_1"), chains["cr1w_1"].n)
                if idx5 >= 1:
                    yE.wait(c5h, chains["c5h"].n)
                yE.op(lambda y, cc=cc:
                      y.dma_start(out=rcol[:], in_=r1_hbm[cc * 128:(cc + 1) * 128, :]),
                      inc=cr1l, k=16)
                vE.wait(cst4, chains["cst4"].n)
                mb = c4 * 4
                ln_stats(mb, cc)
                vE.wait(cr1l, chains["cr1l"].n)
                vE.wait(c5s, chains["c5s"].n)
                vE.op(lambda v, c4=c4, mb=mb:
                      v.tensor_scalar(h1half[:, c4 * DM:(c4 + 1) * DM], rcol[:],
                                      mstat[:, mb:mb + 1], mstat[:, mb + 1:mb + 2],
                                      ALU.subtract, ALU.mult),
                      inc=c5h)
                vE.wait(c5h, chains["c5h"].n)
                for kk in range(8):
                    bnk = 5 + (kk % 2)
                    pbh = kk % 2
                    if idx5 * 8 + kk >= 2:
                        vE.wait(c5t, (idx5 * 8 + kk) - 1)
                    vE.op(lambda v, c4=c4, kk=kk, pbh=pbh:
                          v.tensor_copy(h1bs[pbh][:],
                                        h1half[:, c4 * DM + kk * 512: c4 * DM + (kk + 1) * 512]),
                          inc=c5b)
                    tE.wait(c5b, chains["c5b"].n)
                    if idx5 * 8 + kk >= 2:
                        tE.wait(c5ht, (idx5 * 8 + kk) - 1)
                    for u in range(4):
                        tE.op(lambda t, u=u, bnk=bnk, pbh=pbh:
                              t.transpose(pbf(bnk)[:, u * 128:(u + 1) * 128],
                                          h1bs[pbh][:, u * 128:(u + 1) * 128], idb[:]),
                              inc=c5t if u == 3 else None)
                    vE.wait(c5t, chains["c5t"].n)
                    vE.op(lambda v, kk=kk, c4=c4, bnk=bnk:
                          v.tensor_copy(h1T[:].rearrange("p (k t) -> p k t", k=32)
                                        [:, kk * 4:(kk + 1) * 4, c4 * 128:(c4 + 1) * 128],
                                        pbf(bnk)[:, 0:512]), inc=c5ht)
            tE.wait(cag, chains["cag"].n)
            tE.wait(c5ht, chains["c5ht"].n)
            if hf == 1:
                tE.wait(czsel, chains["czsel"].n)
            for k in range(32):
                tE.op(lambda t, k=k:
                      t.matmul(pb[4][0:36, :], agb[:, k * 36:(k + 1) * 36],
                               h1T[:, k * 512:(k + 1) * 512],
                               start=(k == 0), stop=(k == 31)),
                      inc=czmm if k == 31 else None)
            vE.wait(czmm, chains["czmm"].n)
            vE.op(lambda v: v.tensor_copy(zlog[:], pb[4][0:36, :]), inc=czl)
            vE.wait(czl)
            vE.op(lambda v: v.tensor_scalar(zlog[32:36, :], zlog[32:36, :], eb36[32:36, :],
                                            None, ALU.add), inc=czl)
            for tt in range(4):
                gE.wait(czl, chains["czl"].n)
                gE.wait(czmm, chains["czmm"].n)
                gE.op(lambda g, tt=tt:
                      g.dma_start(out=zst[0:4, :],
                                  in_=zlog[32:36, tt * 128:(tt + 1) * 128]),
                      inc=ch("czt"), k=16)
                tE.wait(ch("czt"), chains["czt"].n)
                if tt >= 1:
                    tE.wait(coh, chains["coh"].n)
                tE.op(lambda t:
                      t.transpose(pb[7][0:128, 0:128], zst[:], idf[:]), inc=czmm)
                vE.wait(czmm, chains["czmm"].n)
                vE.op(lambda v: v.tensor_copy(lgb[:], pb[7][0:128, 0:4]), inc=coh)
                vE.wait(coh)
                vE.op(lambda v: v.tensor_reduce(lmx[:], lgb[:], AX.X, ALU.max), inc=coh)
                vE.wait(coh)
                vE.op(lambda v: v.tensor_scalar(raw4[:], lgb[:], lmx[:], None, ALU.is_equal),
                      inc=coh)
                vE.wait(coh)
                vE.op(lambda v: v.tensor_scalar(nb4[:], raw4[:], -1.0, 1.0, ALU.mult, ALU.add),
                      inc=coh)
                vE.wait(coh)
                vE.op(lambda v: v.tensor_copy(oh4[:, 0:1], raw4[:, 0:1]))
                vE.op(lambda v: v.tensor_tensor(oh4[:, 1:2], raw4[:, 1:2], nb4[:, 0:1], ALU.mult))
                vE.op(lambda v: v.tensor_tensor(oh4[:, 2:3], raw4[:, 2:3], nb4[:, 0:1], ALU.mult))
                vE.op(lambda v: v.tensor_tensor(oh4[:, 3:4], raw4[:, 3:4], nb4[:, 0:1], ALU.mult),
                      inc=coh)
                vE.wait(coh)
                vE.op(lambda v: v.tensor_tensor(oh4[:, 2:3], oh4[:, 2:3], nb4[:, 1:2], ALU.mult))
                vE.op(lambda v: v.tensor_tensor(oh4[:, 3:4], oh4[:, 3:4], nb4[:, 1:2], ALU.mult),
                      inc=coh)
                vE.wait(coh)
                vE.op(lambda v: v.tensor_tensor(oh4[:, 3:4], oh4[:, 3:4], nb4[:, 2:3], ALU.mult),
                      inc=coh)
                vE.wait(coh)
                vE.op(lambda v:
                      v.tensor_tensor(ohx[:].rearrange("p (e r) -> p e r", e=4),
                                      ones32[:].rearrange("p (e r) -> p e r", e=4),
                                      oh4[:].unsqueeze(2).broadcast_to((128, 4, 8)),
                                      ALU.mult), inc=coh)
                tE.wait(coh, chains["coh"].n)
                tE.op(lambda t, tt=tt:
                      t.transpose(pbf(7)[0:32, tt * 128:(tt + 1) * 128], ohx[:], idb[:]),
                      inc=coh)
                vE.wait(coh, chains["coh"].n)
                vE.op(lambda v, tt=tt:
                      v.tensor_copy(ohT[:, tt * 128:(tt + 1) * 128],
                                    pbf(7)[0:32, tt * 128:(tt + 1) * 128]), inc=coh)
            vE.wait(coh, chains["coh"].n)
            vE.wait(czl, chains["czl"].n)
            vE.op(lambda v: v.tensor_tensor(zselT[:], zlog[0:32, :], ohT[:], ALU.mult),
                  inc=czsel)

            for gff in range(8):
                for k in range(32):
                    nn2 = (hf * 8 + gff) * 32 + k
                    if nn2 >= 2:
                        pg, pk = divmod(nn2 - 2, 32)
                        gE.wait(cfmm, pg * 132 + (pk + 1) * 4)
                    gE.op(lambda g, gff=gff, k=k, b=nn2 % 2:
                          g.dma_start(out=wub[b][:],
                                      in_=wupT[k * 128:(k + 1) * 128, gff * 512:(gff + 1) * 512]),
                          inc=ch(f"cwu_{nn2 % 2}"), k=16)
                tE.wait(czsel, chains["czsel"].n)
                for k in range(32):
                    nn2 = (hf * 8 + gff) * 32 + k
                    tE.wait(ch(f"cwu_{nn2 % 2}"), 16 * (nn2 // 2 + 1))
                    if k == 0:
                        tE.wait(cp2e, chains["cp2e"].n)
                    for c4 in range(4):
                        tE.op(lambda t, k=k, c4=c4, b=nn2 % 2:
                              t.matmul(pb[c4][:],
                                       h1T[:, k * 512 + c4 * 128: k * 512 + (c4 + 1) * 128],
                                       wub[b][:], start=(k == 0), stop=(k == 31)),
                              inc=cfmm)
                for c4 in range(4):
                    tE.op(lambda t, gff=gff, c4=c4:
                          t.matmul(pb[4 + c4][:], zselT[:, c4 * 128:(c4 + 1) * 128],
                                   bcb[:, gff * 512:(gff + 1) * 512], start=True, stop=True),
                          inc=cfmm)
                vE.wait(cfmm, chains["cfmm"].n)
                for c4 in range(4):
                    nn3 = (hf * 8 + gff) * 4 + c4
                    cc = hf * 4 + c4
                    if nn3 >= 2:
                        vE.wait(ch(f"cp2w_{nn3 % 2}"), 16 * (nn3 // 2))
                        vE.wait(cst5, 2 * (nn3 - 1))
                    vE.op(lambda v, gff=gff, c4=c4, b=nn3 % 2:
                          v.tensor_tensor(p2b[b][:], pb[4 + c4][:],
                                          h1half[:, c4 * DM + gff * 512: c4 * DM + (gff + 1) * 512],
                                          ALU.add))
                    vE.op(lambda v, c4=c4, b=nn3 % 2:
                          v.tensor_tensor(p2b[b][:], p2b[b][:], pb[c4][:], ALU.add),
                          inc=cp2e)
                    vE.wait(cp2e)
                    vE.op(lambda v, gff=gff, cc=cc, b=nn3 % 2:
                          v.tensor_reduce(statS[:, cc * 8 + gff: cc * 8 + gff + 1],
                                          p2b[b][:], AX.X, ALU.add), inc=cst5)
                    sE.wait(cp2e, chains["cp2e"].n)
                    sE.op(lambda s, gff=gff, cc=cc, b=nn3 % 2:
                          s.activation(trash5[:], p2b[b][:], AF.Square,
                                       accum_out=statQ[:, cc * 8 + gff: cc * 8 + gff + 1]),
                          inc=cst5)
                    yE.wait(cst5, chains["cst5"].n)
                    yE.op(lambda y, gff=gff, c4=c4, b=nn3 % 2:
                          y.dma_start(out=p2_hbm[c4 * 128:(c4 + 1) * 128,
                                                 gff * 512:(gff + 1) * 512],
                                      in_=p2b[b][:]), inc=ch(f"cp2w_{nn3 % 2}"), k=16)
            for c4 in range(4):
                cc = hf * 4 + c4
                mb = c4 * 4
                vE.wait(cst5, chains["cst5"].n)
                ln_stats(mb, cc)
                yE.wait(ch("cp2w_0"), chains["cp2w_0"].n)
                yE.wait(ch("cp2w_1"), chains["cp2w_1"].n)
                if hf * 4 + c4 >= 1:
                    yE.wait(c5o, chains["c5o"].n)
                    yE.wait(cfin, chains["cfin"].n)
                yE.op(lambda y, c4=c4:
                      y.dma_start(out=rcol[:], in_=p2_hbm[c4 * 128:(c4 + 1) * 128, :]),
                      inc=cp2l, k=16)
                vE.wait(cp2l, chains["cp2l"].n)
                vE.wait(c5s, chains["c5s"].n)
                vE.op(lambda v, mb=mb:
                      v.tensor_scalar(rcol[:], rcol[:],
                                      mstat[:, mb:mb + 1], mstat[:, mb + 1:mb + 2],
                                      ALU.subtract, ALU.mult),
                      inc=c5o)
                yE.wait(c5o, chains["c5o"].n)
                yE.op(lambda y, cc=cc:
                      y.dma_start(out=yout[cc * 128:(cc + 1) * 128, :], in_=rcol[:]),
                      inc=cfin, k=16)

    if upto >= 6:
        yE.wait(chains["cfin"].h, None) if False else yE.wait(ch("cfin"), chains["cfin"].n)
    if dbg:
        cdbg = ch("cdbg")
        yE.wait(ch("cschunk"), chains["cschunk"].n)
        dump_list = [("khatT", khatT), ("krT", krT), ("vtm", vtm), ("etot", etot),
                     ("schunk", schunk)]
        if upto <= 2:
            dump_list += [("eahalf", eahalf), ("uarB", uarB)]
        for nm, t_ in dump_list:
            od = nc.dram_tensor("dbg_" + nm, list(t_.ap().shape), t_.ap().dtype,
                                kind="ExternalOutput").ap()
            yE.op(lambda y, od=od, t_=t_: y.dma_start(out=od, in_=t_[:]),
                  inc=cdbg, k=16)
        if upto >= 3:
            yE.wait(ch("cqw"), chains["cqw"].n)
            for nm, t_ in [("qt", qt_hbm), ("sqt", sqt_hbm)]:
                od = nc.dram_tensor("dbg_" + nm, [KVH * 128, NCH * 512], BF16,
                                    kind="ExternalOutput").ap()
                yE.op(lambda y, od=od, t_=t_:
                      y.dma_start(out=od.rearrange("(a p) c -> a p c", a=KVH), in_=t_),
                      inc=cdbg, k=16)
        if upto >= 4:
            yE.wait(ch("cow"), chains["cow"].n)
            od = nc.dram_tensor("dbg_oh", [KVH * 128, 4 * T], BF16,
                                kind="ExternalOutput").ap()
            yE.op(lambda y, od=od:
                  y.dma_start(out=od.rearrange("(a p) c -> a p c", a=KVH), in_=o_hbm),
                  inc=cdbg, k=16)
        if upto >= 5:
            yE.wait(ch("cr1"), chains["cr1"].n)
            if sub5 >= 4:
                yE.wait(ch("cr1w_0"), chains["cr1w_0"].n)
                yE.wait(ch("cr1w_1"), chains["cr1w_1"].n)
            if sub5 >= 5:
                od = nc.dram_tensor("dbg_r1", [T, DM], F32, kind="ExternalOutput").ap()
                yE.op(lambda y, od=od: y.dma_start(out=od, in_=r1_hbm), inc=cdbg, k=16)
            if sub5 >= 2:
                yE.wait(ch("cst4"), chains["cst4"].n)
                od2 = nc.dram_tensor("dbg_stat", [128, 128], F32, kind="ExternalOutput").ap()
                yE.op(lambda y, od2=od2: y.dma_start(out=od2[:, 0:64], in_=statS[:]),
                      inc=cdbg, k=16)
                yE.op(lambda y, od2=od2: y.dma_start(out=od2[:, 64:128], in_=statQ[:]),
                      inc=cdbg, k=16)
        if upto <= 2:
            od = nc.dram_tensor("dbg_relay", [128, 128], F32, kind="ExternalOutput").ap()
            yE.op(lambda y, od=od: y.dma_start(out=od[:, 0:64], in_=eall[:]), inc=cdbg, k=16)
            yE.op(lambda y, od=od: y.dma_start(out=od[:, 64:72], in_=ppb[:]), inc=cdbg, k=16)
            yE.op(lambda y, od=od: y.dma_start(out=od[:, 72:80], in_=pps[:]), inc=cdbg, k=16)
            yE.op(lambda y, od=od: y.dma_start(out=od[:, 80:96], in_=selC[:]), inc=cdbg, k=16)
            od = nc.dram_tensor("dbg_sj", [128, 1024], F32, kind="ExternalOutput").ap()
            yE.op(lambda y, od=od: y.dma_start(out=od, in_=sj[0][:]), inc=cdbg, k=16)
            od = nc.dram_tensor("dbg_ccout", [NCORE * 1152, 128], F32,
                                kind="ExternalOutput").ap()
            yE.op(lambda y, od=od: y.dma_start(out=od, in_=cc_out), inc=cdbg, k=16)
            od = nc.dram_tensor("dbg_sinb", [128, KVH * 128], F32,
                                kind="ExternalOutput").ap()
            yE.op(lambda y, od=od: y.dma_start(out=od, in_=sinb[:]), inc=cdbg, k=16)
        yE.wait(cdbg, chains["cdbg"].n)

    with nc.Block() as block:
        block.tensor(lambda e: [f(e) for f in PROG["tensor"]])
        block.vector(lambda e: [f(e) for f in PROG["vector"]])
        block.scalar(lambda e: [f(e) for f in PROG["scalar"]])
        block.sync(lambda e: [f(e) for f in PROG["sync"]])
        block.gpsimd(lambda e: [f(e) for f in PROG["gpsimd"]])
    es.close()
    return nc


_NC_CACHE = [None]


def _consts_for_core(core):
    qh = core % 4
    start = qh * T
    pos = np.arange(start - 64, start + T).astype(np.float32)
    half = np.arange(D // 2, dtype=np.float32)
    inv = 1.0 / 10000.0 ** (2.0 * half / D)
    fr = pos[:, None] * inv[None, :]
    emb = np.concatenate([fr, fr], axis=-1)
    cos = np.cos(emb).astype(np.float32)
    sin = np.sin(emb).astype(np.float32)
    cosL = np.zeros((128, 9 * 128), np.float32)
    sinL = np.zeros((128, 9 * 128), np.float32)
    cosL[0:64, 0:128] = cos[0:64]
    sinL[0:64, 0:128] = sin[0:64]
    for c in range(NCH):
        cosL[:, (c + 1) * 128:(c + 2) * 128] = cos[64 + c * 128: 64 + (c + 1) * 128]
        sinL[:, (c + 1) * 128:(c + 2) * 128] = sin[64 + c * 128: 64 + (c + 1) * 128]
    cosin = np.concatenate([cosL, sinL], axis=1)

    identb = np.eye(128, dtype=np.float32).astype(ml_dtypes.bfloat16)
    identf = np.eye(128, dtype=np.float32)
    tri = (np.arange(128)[:, None] <= np.arange(128)[None, :]).astype(np.float32)
    glam = np.concatenate([tri, tri, tri, tri], axis=1).astype(ml_dtypes.bfloat16)

    aa = np.arange(64)
    jj = np.arange(128)
    mrest = np.where((jj[None, :] >= aa[:, None]) & (jj[None, :] <= aa[:, None] + 64),
                     0.0, -1e30).astype(np.float32)
    mrest2 = np.concatenate([mrest, mrest], axis=0)
    if qh == 0:
        mfirst = np.where(jj[None, :] >= 64, mrest, -1e30).astype(np.float32)
        mfirst2 = np.concatenate([mfirst, mfirst], axis=0)
    else:
        mfirst2 = mrest2
    swm = np.concatenate([mfirst2, mrest2], axis=1)

    bstart = 4 * (core // 4)
    sel = np.zeros(8, np.float32)
    for j in range(8):
        sel[j] = 1.0 if (bstart <= j < core) else 0.0
    selc = np.zeros((128, 16), np.float32)
    selc[:, 0:8] = sel[None, :]
    selc[:, 8:16] = 1.0 - sel[None, :]
    zeros = np.zeros((128, 128), np.float32)
    fconst = np.zeros((128, 4), np.float32)
    fconst[:, 0] = LN_HALF
    fconst[:, 1] = 1.0
    fconst[:, 2] = EPS
    ones = np.ones((128, 32), np.float32).astype(ml_dtypes.bfloat16)
    return cosin, identb, identf, glam, swm, selc, zeros, fconst, ones


def kernel(L0_attn_hidden_states, L0_attn_Wq, L0_attn_Wk, L0_attn_Wv, L0_attn_Wo,
           L0_ffn_W_up, L0_ffn_gate_w, L0_ffn_expert_bias, L0_ffn_lora_A, L0_ffn_lora_B):
    X = np.ascontiguousarray(L0_attn_hidden_states, dtype=np.float32)
    wupT = np.ascontiguousarray(L0_ffn_W_up.T)
    A_cat = np.ascontiguousarray(L0_ffn_lora_A.transpose(2, 0, 1).reshape(DM, 32))
    agw = np.ascontiguousarray(np.concatenate([A_cat, L0_ffn_gate_w.T], axis=1))
    bcat = np.ascontiguousarray(L0_ffn_lora_B.transpose(0, 2, 1).reshape(32, DM))
    eb = np.zeros((36, 1), np.float32)
    eb[32:36, 0] = np.asarray(L0_ffn_expert_bias, dtype=np.float32)

    if _NC_CACHE[0] is None:
        _NC_CACHE[0] = build_program()
    nc = _NC_CACHE[0]

    in_maps = []
    for core in range(NCORE):
        b, qh = core // 4, core % 4
        s = qh * T
        halo = np.zeros((64, DM), np.float32) if qh == 0 else X[b, s - 64:s]
        xs = np.concatenate([halo, X[b, s:s + T]], axis=0)
        cosin, identb, identf, glam, swm, selc, zeros, fconst, ones = _consts_for_core(core)
        in_maps.append({
            "xin": np.ascontiguousarray(xs),
            "wq": L0_attn_Wq, "wk": L0_attn_Wk, "wv": L0_attn_Wv, "wo": L0_attn_Wo,
            "wupT": wupT, "agw": agw, "bcat": bcat, "ebias": eb,
            "cosin": cosin, "identb": identb, "identf": identf, "glam": glam,
            "swm_in": swm, "sel_in": selc, "zeros_in": zeros,
            "fconst_in": fconst, "ones_in": ones,
        })
    res = run_bass_kernel_spmd(nc, in_maps, core_ids=list(range(NCORE)))
    out = np.empty((2, NT, DM), np.float32)
    for core in range(NCORE):
        b, qh = core // 4, core % 4
        out[b, qh * T:(qh + 1) * T, :] = res.results[core]["y"]
    return out
